# version 03 (final)
# baseline (speedup 1.0000x reference)
import math
import contextlib
import numpy as np
import ml_dtypes
import concourse.bass as bass
import concourse.mybir as mybir
from concourse.bass_utils import run_bass_kernel_spmd

F32 = mybir.dt.float32
BF16 = mybir.dt.bfloat16
AF = mybir.ActivationFunctionType
ALU = mybir.AluOpType
AX = mybir.AxisListType
BF = ml_dtypes.bfloat16

NCORE = 8
B = 16
BPC = 2
T = 2048
D = 1024
NH = 16
HD = 64
NT = T // 128
NC = D // 128
EPS_LN = 1e-5
_EPS1 = math.exp(-1.0 / 12.0)
EPS_R1 = _EPS1 / 2.0
EPS_K1 = _EPS1 + 0.3 / 11.0
EPS_V1 = _EPS1
_PARANOID = False
_PAR2 = False
_BLIM = {}
_CLIM = {}
_FLIM = {}


class Ctx:

    def __init__(self, nc, es):
        self.nc = nc
        self.es = es
        self.stage_es = es
        self._sems = {}
        self._idvals = {}
        self._persist = {}
        self.pfx = ""
        self.stage_idx = 0
        self.lastcast = None

    def sb(self, name, shape, dtype):
        return self.stage_es.enter_context(self.nc.sbuf_tensor(self.pfx + name, shape, dtype))

    def ps(self, name, shape, dtype):
        return self.stage_es.enter_context(self.nc.psum_tensor(self.pfx + name, shape, dtype))

    def prec(self, name):
        if name not in self._persist:
            h = self.es.enter_context(self.nc.semaphore(name))
            self._persist[name] = [h, 0]
        return self._persist[name]

    def stage_begin(self, idx, pfx):
        import contextlib as _cl
        self.stage_idx = idx
        self.pfx = pfx
        self.stage_es = _cl.ExitStack()
        self._sems = {}
        self.lastcast = None
        if idx > 0:
            bar = self.prec("BAR")
            block = self.ncblock
            for engname in ("tensor", "scalar", "vector", "sync", "gpsimd"):
                def _w(eng, bar=bar, idx=idx):
                    eng.wait_ge(bar[0], idx)
                getattr(block, engname)(_w)

    def stage_end(self):
        block = self.ncblock
        engd = self.prec("ENGD")
        bar = self.prec("BAR")
        for engname in ("tensor", "scalar", "vector", "sync"):
            def _n(eng, engd=engd):
                eng.nop().then_inc(engd[0], 1)
                engd[1] += 1
            getattr(block, engname)(_n)
        tgt = engd[1]

        def _fin(g, tgt=tgt, engd=engd, bar=bar):
            g.wait_ge(engd[0], tgt)
            g.nop().then_inc(bar[0], 1)
            bar[1] += 1
        block.gpsimd(_fin)
        for (h, cnt, base) in self._sems.values():
            self._idvals[h.num] = cnt
        self.stage_es.close()
        self._sems = {}

    def rec(self, name):
        if name not in self._sems:
            h = self.stage_es.enter_context(self.nc.semaphore(self.pfx + name))
            base = self._idvals.get(h.num, 0)
            self._sems[name] = [h, base, base]
        return self._sems[name]

    def inc(self, instr, name, k=1):
        r = self.rec(name)
        instr.then_inc(r[0], k)
        r[1] += k
        return r[1]

    def v(self, name):
        r = self.rec(name)
        return r[1] - r[2]

    def wait(self, eng, name, val=None):
        r = self.rec(name)
        eng.wait_ge(r[0], r[1] if val is None else r[2] + val)



def _prep_host(ins):
    h = {}
    f32c = lambda a: np.ascontiguousarray(np.asarray(a), dtype=np.float32)
    tr = lambda a: np.ascontiguousarray(f32c(a).T)
    cols = lambda a: np.ascontiguousarray(f32c(a).reshape(NC, 128).T)
    h["wqT"], h["wkT"], h["wvT"], h["woT"] = (tr(ins["L0_attn_Wq"]), tr(ins["L0_attn_Wk"]),
                                              tr(ins["L0_attn_Wv"]), tr(ins["L0_attn_Wo"]))
    h["bq_c"], h["bk_c"] = cols(ins["L0_attn_bq"]), cols(ins["L0_attn_bk"])
    h["bv_r"] = f32c(ins["L0_attn_bv"]).reshape(1, D)
    h["bo_r"] = f32c(ins["L0_attn_bo"]).reshape(1, D)
    h["etg"] = f32c(ins["L0_attn_et_gate"]).reshape(1, 1)
    h["isg"] = f32c(ins["L0_attn_is_gate"]).reshape(1, 1)
    tb = f32c(ins["L0_attn_time_bias"]).ravel()
    kk = np.arange(T)[:, None]
    qq = np.arange(T)[None, :]
    h["tpT"] = np.ascontiguousarray(np.take(tb, np.clip(kk - qq + 250, 0, 499)).astype(np.float32))
    h["router_w"] = f32c(ins["L0_ffn_router_w"])
    h["w_gate"], h["w_up"], h["w_down"] = f32c(ins["L0_ffn_w_gate"]), f32c(ins["L0_ffn_w_up"]), f32c(ins["L0_ffn_w_down"])
    h["ws_gate"], h["ws_up"], h["ws_down"] = f32c(ins["L0_ffn_ws_gate"]), f32c(ins["L0_ffn_ws_up"]), f32c(ins["L0_ffn_ws_down"])
    h["wrT2"], h["wkT2"], h["wvT2"], h["woT2"] = (tr(ins["L1_attn_Wr"]), tr(ins["L1_attn_Wk"]),
                                                  tr(ins["L1_attn_Wv"]), tr(ins["L1_attn_Wo"]))
    h["br_c2"], h["bk_c2"], h["bv_c2"] = cols(ins["L1_attn_br"]), cols(ins["L1_attn_bk"]), cols(ins["L1_attn_bv"])
    h["bo_r2"] = f32c(ins["L1_attn_bo"]).reshape(1, D)
    h["td_c"], h["u_c"] = cols(ins["L1_attn_time_decay"]), cols(ins["L1_attn_U"])
    h["wrT3"], h["wkT3"], h["wvT3"] = tr(ins["L1_ffn_W_r"]), tr(ins["L1_ffn_W_k"]), tr(ins["L1_ffn_W_v"])
    h["mur_c"], h["muk_c"] = cols(ins["L1_ffn_mu_r"]), cols(ins["L1_ffn_mu_k"])
    h["ident_bf"] = np.eye(128, dtype=BF)
    h["trem_row"] = (np.arange(T, dtype=np.float32) - (T - 1.0)).reshape(1, T)
    h["x"] = f32c(ins["L0_attn_x"])
    return h



def emit_ln_to_fm(ctx, block, tag, nt, src_dma, xnfm_dst, tmp, id_s, extra_pe_wait=None, prev_tag=None):
    x_t, sq_t, st, xnb, ps_trb = tmp["x"], tmp["sq"], tmp["st"], tmp["xnb"], tmp["ps_trb"]

    @ctx.ncblock.sync
    def _(sync):
        for tt in range(nt):
            if tt >= 2:
                ctx.wait(sync, f"{tag}xn", tt - 1)
            elif prev_tag is not None:
                ctx.wait(sync, f"{prev_tag}xn", ctx.v(f"{prev_tag}xn") - 1 + tt)
            ctx.inc(src_dma(sync, tt, x_t[tt % 2]), f"{tag}ldx", 16)

    @ctx.ncblock.scalar
    def _(s):
        for tt in range(nt + 2):
            if tt < nt:
                i = tt % 2
                ctx.wait(s, f"{tag}ldx", 16 * (tt + 1))
                if tt >= 2:
                    ctx.wait(s, f"{tag}s0", tt - 1)
                elif prev_tag is not None:
                    ctx.wait(s, f"{prev_tag}s0", ctx.v(f"{prev_tag}s0") - 1 + tt)
                ctx.inc(s.activation(sq_t[i][:, :], x_t[i][:, :], AF.Square), f"{tag}sq", 1)
            if 1 <= tt <= nt:
                j = (tt - 1) % 2
                if tt == 1:
                    ctx.wait(s, "prep", 1)
                ctx.wait(s, f"{tag}s1", tt)
                s.activation(st[j][:, 6:7], st[j][:, 5:6], AF.Ln, bias=tmp["epscol"][:, :])
                ctx.inc(s.activation(st[j][:, 7:8], st[j][:, 6:7], AF.Exp, scale=-0.5), f"{tag}s2", 1)
            if tt >= 2:
                for c in range(NC):
                    n = (tt - 2) * NC + c
                    if n % 4 == 0:
                        ctx.wait(s, f"{tag}trd", n + 4)
                    bk = (n // 4) % 2
                    sl = n % 4
                    ctx.inc(s.activation(xnfm_dst(c)[:, (tt - 2) * 128:(tt - 1) * 128],
                                         ps_trb[bk][:, sl * 128:(sl + 1) * 128], AF.Copy), f"{tag}trev", 1)

    @ctx.ncblock.vector
    def _(v):
        for tt in range(nt + 1):
            if tt < nt:
                i = tt % 2
                ctx.wait(v, f"{tag}sq", tt + 1)
                v.reduce_sum(st[i][:, 0:1], x_t[i][:, :], axis=AX.X)
                ctx.inc(v.reduce_sum(st[i][:, 1:2], sq_t[i][:, :], axis=AX.X), f"{tag}s0", 1)
                ctx.wait(v, f"{tag}s0", tt + 1)
                v.tensor_scalar(st[i][:, 2:3], st[i][:, 0:1], 1.0 / D, None, ALU.mult)
                ctx.inc(v.tensor_scalar(st[i][:, 3:4], st[i][:, 1:2], 1.0 / D, None, ALU.mult), f"{tag}sm", 1)
                ctx.wait(v, f"{tag}sm", tt + 1)
                ctx.inc(v.tensor_tensor(st[i][:, 4:5], st[i][:, 2:3], st[i][:, 2:3], ALU.mult), f"{tag}sv", 1)
                ctx.wait(v, f"{tag}sv", tt + 1)
                ctx.inc(v.tensor_tensor(st[i][:, 5:6], st[i][:, 3:4], st[i][:, 4:5], ALU.subtract), f"{tag}s1", 1)
            if tt >= 1:
                j = (tt - 1) % 2
                ctx.wait(v, f"{tag}s2", tt)
                ctx.inc(v.tensor_scalar(st[j][:, 4:5], st[j][:, 2:3], st[j][:, 7:8], -1.0, ALU.mult, ALU.mult), f"{tag}s3", 1)
                ctx.wait(v, f"{tag}s3", tt)
                if tt - 1 >= 2:
                    ctx.wait(v, f"{tag}trd", (tt - 2) * NC)
                ctx.inc(v.tensor_scalar(xnb[j][:, :], x_t[j][:, :], st[j][:, 7:8], st[j][:, 4:5], ALU.mult, ALU.add),
                        f"{tag}xn", 1)

    @ctx.ncblock.tensor
    def _(t):
        if extra_pe_wait is not None:
            extra_pe_wait(t)
        if prev_tag is not None:
            ctx.wait(t, f"{prev_tag}trev", ctx.v(f"{prev_tag}trev"))
        for tt in range(nt):
            i = tt % 2
            ctx.wait(t, f"{tag}xn", tt + 1)
            for c in range(NC):
                n = tt * NC + c
                if _PARANOID and n >= 1:
                    ctx.wait(t, f"{tag}trev", n)
                elif n % 4 == 0 and n >= 8:
                    ctx.wait(t, f"{tag}trev", n - 4)
                bk = (n // 4) % 2
                sl = n % 4
                ctx.inc(t.transpose(ps_trb[bk][:, sl * 128:(sl + 1) * 128],
                                    xnb[i][:, c * 128:(c + 1) * 128], id_s[:, :]), f"{tag}trd", 1)


def emit_weight_load(ctx, block, tag, wdram_chunks, w_sb, wstg, pe_free_count=None, pe_free_name=None,
                     prev_cast=None):
    n = len(wdram_chunks)
    nstg = len(wstg)

    @ctx.ncblock.sync
    def _(sync):
        for c in range(n):
            if c >= nstg:
                ctx.wait(sync, f"{tag}wc", c - nstg + 1)
            elif prev_cast is not None:
                ctx.wait(sync, prev_cast[0], prev_cast[1])
            ctx.inc(sync.dma_start(wstg[c % nstg][:, :], wdram_chunks[c]), f"{tag}wl", 16)

    @ctx.ncblock.scalar
    def _(se):
        if pe_free_name is not None:
            ctx.wait(se, pe_free_name, pe_free_count)
        for c in range(n):
            ctx.wait(se, f"{tag}wl", 16 * (c + 1))
            ctx.inc(se.activation(w_sb[c][:, :], wstg[c % nstg][:, :], AF.Copy), f"{tag}wc", 1)



def decl_A(nc, d):
    if "xin" not in d:
        d["xin"] = nc.dram_tensor("xin", [BPC, T, D], F32, kind="ExternalInput")
    if "wqT" not in d:
        d["wqT"] = nc.dram_tensor("wqT", [D, D], F32, kind="ExternalInput")
    if "wkT" not in d:
        d["wkT"] = nc.dram_tensor("wkT", [D, D], F32, kind="ExternalInput")
    if "wvT" not in d:
        d["wvT"] = nc.dram_tensor("wvT", [D, D], F32, kind="ExternalInput")
    if "bq_c" not in d:
        d["bq_c"] = nc.dram_tensor("bq_c", [128, NC], F32, kind="ExternalInput")
    if "bk_c" not in d:
        d["bk_c"] = nc.dram_tensor("bk_c", [128, NC], F32, kind="ExternalInput")
    if "bv_r" not in d:
        d["bv_r"] = nc.dram_tensor("bv_r", [1, D], F32, kind="ExternalInput")
    if "ident_bf" not in d:
        d["ident_bf"] = nc.dram_tensor("ident_bf", [128, 128], BF16, kind="ExternalInput")
    if "xn0fm" not in d:
        d["xn0fm"] = nc.dram_tensor("xn0fm", [BPC, NC, 128, T], BF16, kind="ExternalOutput")
    if "qtf" not in d:
        d["qtf"] = nc.dram_tensor("qtf", [BPC, NC, 128, T], BF16, kind="ExternalOutput")
    if "ktf" not in d:
        d["ktf"] = nc.dram_tensor("ktf", [BPC, NC, 128, T], BF16, kind="ExternalOutput")
    if "vaug" not in d:
        d["vaug"] = nc.dram_tensor("vaug", [BPC, NT, 128, NH * 65], BF16, kind="ExternalOutput")


def emit_A(nc, ctx, block, d):
    xin = d["xin"]
    wqT = d["wqT"]
    wkT = d["wkT"]
    wvT = d["wvT"]
    bq_c = d["bq_c"]
    bk_c = d["bk_c"]
    bv_r = d["bv_r"]
    ident = d["ident_bf"]
    xn0fm = d["xn0fm"]
    qtf = d["qtf"]
    ktf = d["ktf"]
    vaug = d["vaug"]

    sb = ctx.sb
    x_t = [sb(f"sx{i}", [128, D], F32) for i in range(2)]
    sq_t = [sb(f"ssq{i}", [128, D], F32) for i in range(2)]
    st = [sb(f"sst{i}", [128, 8], F32) for i in range(2)]
    xnb = [sb(f"sxnb{i}", [128, D], BF16) for i in range(2)]
    xnfm = [[sb(f"sxnfm{b}_{c}", [128, T], BF16) for c in range(NC)] for b in range(BPC)]
    id_s = sb("sid", [128, 128], BF16)
    wstg = [sb(f"swstg{i}", [128, D], F32) for i in range(2)]
    w_sb = [sb(f"sw{c}", [128, D], BF16) for c in range(NC)]
    bqc_s = sb("sbqc", [128, NC], F32)
    bkc_s = sb("sbkc", [128, NC], F32)
    bq8_s = sb("sbq8", [128, NC], F32)
    bvr_s = sb("sbvr", [1, D], F32)
    ones_r = sb("sones", [1, 128], F32)
    ev_t = [sb(f"sev{i}", [128, 512], BF16) for i in range(2)]
    vstg = [sb(f"svstg{i}", [128, NH * 65], BF16) for i in range(2)]

    epscol = sb("sepscol", [128, 1], F32)
    ps_trb = [ctx.ps(f"ptrb{i}", [128, 1024], BF16) for i in range(2)]
    ps_q = [ctx.ps(f"pq{i}", [128, 512], F32) for i in range(2)]
    ps_v = [ctx.ps(f"pv{i}", [128, 1024], F32) for i in range(2)]


    @block.sync
    def _(sync):
        ctx.inc(sync.dma_start(id_s[:, :], ident[:, :]), "ld0", 16)
        ctx.inc(sync.dma_start(bqc_s[:, :], bq_c[:, :]), "ld0", 16)
        ctx.inc(sync.dma_start(bkc_s[:, :], bk_c[:, :]), "ld0", 16)
        ctx.inc(sync.dma_start(bvr_s[:, :], bv_r[:, :]), "ld0", 16)

    @block.vector
    def _(v):
        ctx.wait(v, "ld0")
        v.memset(ones_r[:, :], 1.0)
        v.memset(vstg[0][:, 64::65], 1.0)
        v.memset(vstg[1][:, 64::65], 1.0)
        v.memset(epscol[:, :], EPS_LN)
        ctx.inc(v.tensor_scalar(bq8_s[:, :], bqc_s[:, :], 0.125, None, ALU.mult), "prep", 1)

    tmp = {"x": x_t, "sq": sq_t, "st": st, "xnb": xnb, "ps_trb": ps_trb, "epscol": epscol}

    for b in range(BPC):
        def srcd(sync, tt, buf, b=b):
            return sync.dma_start(buf[:, :], xin[b, tt * 128:(tt + 1) * 128, :])

        def pe_wait(t, b=b):
            ctx.wait(t, "ld0")

        emit_ln_to_fm(ctx, block, f"A{b}", NT, srcd, lambda c, b=b: xnfm[b][c], tmp, id_s,
                      extra_pe_wait=pe_wait, prev_tag=(f"A{b-1}" if b > 0 else None))

        @block.gpsimd
        def _(g, b=b):
            ctx.wait(g, f"A{b}trev", NT * NC)
            for c in range(NC):
                ctx.inc(g.dma_start(xn0fm[b, c, :, :], xnfm[b][c][:, :]), "sto", 16)

    projs = [("q", wqT, qtf, bq8_s, 0.125), ("k", wkT, ktf, bkc_s, 1.0)]
    prev_pe_done = (None, None)
    prev_cast = None
    for (pname, wdram, outdram, bias_s, scale) in projs:
        chunks = [wdram[c * 128:(c + 1) * 128, :] for c in range(NC)]
        emit_weight_load(ctx, block, pname, chunks, w_sb, wstg,
                         pe_free_count=prev_pe_done[1], pe_free_name=prev_pe_done[0],
                         prev_cast=prev_cast)
        prev_cast = (f"{pname}wc", NC)

        @block.tensor
        def _(t, pname=pname):
            ctx.wait(t, f"{pname}wc", NC)
            for b in range(BPC):
                ctx.wait(t, f"A{b}trev", NT * NC)
            for b in range(BPC):
                for dq in range(NC):
                    for sl in range(4):
                        n = (b * NC + dq) * 4 + sl
                        if n >= 2:
                            ctx.wait(t, f"{pname}ev", n - 1)
                        for c in range(NC):
                            last = (c == NC - 1)
                            m = t.matmul(ps_q[n % 2][:, :], w_sb[c][:, dq * 128:(dq + 1) * 128],
                                         xnfm[b][c][:, sl * 512:(sl + 1) * 512],
                                         start=(c == 0), stop=last)
                            if last:
                                ctx.inc(m, f"{pname}mm", 1)

        @block.scalar
        def _(s, pname=pname, bias_s=bias_s, scale=scale):
            if pname == "k":
                ctx.wait(s, "qst")
            for b in range(BPC):
                for dq in range(NC):
                    for sl in range(4):
                        n = (b * NC + dq) * 4 + sl
                        ctx.wait(s, f"{pname}mm", n + 1)
                        if n >= 2:
                            ctx.wait(s, f"{pname}st", 16 * (n - 1))
                        ctx.inc(s.activation(ev_t[n % 2][:, :], ps_q[n % 2][:, :], AF.Identity,
                                             bias=bias_s[:, dq:dq + 1], scale=scale), f"{pname}ev", 1)

        @block.gpsimd
        def _(g, pname=pname, outdram=outdram):
            for b in range(BPC):
                for dq in range(NC):
                    for sl in range(4):
                        n = (b * NC + dq) * 4 + sl
                        ctx.wait(g, f"{pname}ev", n + 1)
                        ctx.inc(g.dma_start(outdram[b, dq, :, sl * 512:(sl + 1) * 512],
                                            ev_t[n % 2][:, :]), f"{pname}st", 16)

        prev_pe_done = (f"{pname}mm", BPC * NC * 4)

    chunks = [wvT[c * 128:(c + 1) * 128, :] for c in range(NC)]
    emit_weight_load(ctx, block, "v", chunks, w_sb, wstg,
                     pe_free_count=prev_pe_done[1], pe_free_name=prev_pe_done[0],
                     prev_cast=prev_cast)

    @block.tensor
    def _(t):
        ctx.wait(t, "vwc", NC)
        ctx.wait(t, "prep", 1)
        for b in range(BPC):
            for tt in range(NT):
                n = b * NT + tt
                if n >= 2:
                    ctx.wait(t, "vev", n - 1)
                for hN in range(2):
                    for c in range(NC):
                        t.matmul(ps_v[n % 2][:, hN * 512:(hN + 1) * 512],
                                 xnfm[b][c][:, tt * 128:(tt + 1) * 128],
                                 w_sb[c][:, hN * 512:(hN + 1) * 512],
                                 start=(c == 0), stop=False)
                    last = t.matmul(ps_v[n % 2][:, hN * 512:(hN + 1) * 512], ones_r[:, :],
                                    bvr_s[:, hN * 512:(hN + 1) * 512], start=False, stop=True)
                ctx.inc(last, "vmm", 1)

    @block.scalar
    def _(s):
        for b in range(BPC):
            for tt in range(NT):
                n = b * NT + tt
                ctx.wait(s, "vmm", n + 1)
                if n >= 2:
                    ctx.wait(s, "vst", 16 * (n - 1))
                out_ap = vstg[n % 2][:, :].rearrange("p (h c) -> p h c", h=NH)[:, :, 0:64]
                in_ap = ps_v[n % 2][:, :].rearrange("p (h c) -> p h c", h=NH)
                ctx.inc(s.activation(out_ap, in_ap, AF.Copy), "vev", 1)

    @block.gpsimd
    def _(g):
        for b in range(BPC):
            for tt in range(NT):
                n = b * NT + tt
                ctx.wait(g, "vev", n + 1)
                ctx.inc(g.dma_start(vaug[b, tt, :, :], vstg[n % 2][:, :]), "vst", 16)

    @block.gpsimd
    def _(g):
        ctx.wait(g, "sto")
        ctx.wait(g, "vst")
        ctx.wait(g, "qst")
        ctx.wait(g, "kst")



def build_A():
    nc = bass.Bass("TRN2", target_bir_lowering=False)
    es = contextlib.ExitStack()
    d = {}
    decl_A(nc, d)
    with nc.Block() as block:
        ctx = Ctx(nc, es)
        ctx.ncblock = block
        emit_A(nc, ctx, block, d)
    return nc, es



def decl_B(nc, d):
    if "qtf" not in d:
        d["qtf"] = nc.dram_tensor("qtf", [BPC, NC, 128, T], BF16, kind="ExternalInput")
    if "ktf" not in d:
        d["ktf"] = nc.dram_tensor("ktf", [BPC, NC, 128, T], BF16, kind="ExternalInput")
    if "vaug" not in d:
        d["vaug"] = nc.dram_tensor("vaug", [BPC, NT, 128, NH * 65], BF16, kind="ExternalInput")
    if "tpT" not in d:
        d["tpT"] = nc.dram_tensor("tpT", [T, T], F32, kind="ExternalInput")
    if "etg" not in d:
        d["etg"] = nc.dram_tensor("etg", [1, 1], F32, kind="ExternalInput")
    if "isg" not in d:
        d["isg"] = nc.dram_tensor("isg", [1, 1], F32, kind="ExternalInput")
    if "ctxo" not in d:
        d["ctxo"] = nc.dram_tensor("ctxo", [BPC, NH, T, HD], BF16, kind="ExternalOutput")


def emit_B(nc, ctx, block, d):
    qtf = d["qtf"]
    ktf = d["ktf"]
    vaug = d["vaug"]
    tpT = d["tpT"]
    etg = d["etg"]
    isg = d["isg"]
    ctxo = d["ctxo"]

    sb = ctx.sb
    qt_sb = [sb(f"bq{c}", [128, T], BF16) for c in range(NC)]
    kt_sb = [sb(f"bk{c}", [128, T], BF16) for c in range(NC)]
    va_sb = [sb(f"bv{t}", [128, NH * 65], BF16) for t in range(NT)]
    tp_f = [sb(f"btp{i}", [128, 512], F32) for i in range(2)]
    tp2_f = [sb(f"btp2{i}", [128, 512], F32) for i in range(2)]
    etp = [sb(f"betp{k}", [128, 512], BF16) for k in range(16)]
    e_sb = [sb(f"be{i}", [128, 512], BF16) for i in range(2)]
    pt = [[sb(f"bpt{i}_{k}", [128, 512], BF16) for k in range(16)] for i in range(2)]
    sgig = sb("bsgig", [128, 2], F32)
    sg11 = sb("bsg11", [1, 2], F32)
    ones_r = sb("bones", [1, 128], F32)
    ctx_sb = [sb(f"bctx{i}", [128, 64], BF16) for i in range(2)]
    rc = sb("brc", [128, 4], F32)

    ps_s = [ctx.ps(f"bps{i}", [128, 512], F32) for i in range(2)]
    ps_c = [ctx.ps(f"bpc{i}", [128, 65], F32) for i in range(2)]
    ps_b = ctx.ps("bpb", [128, 2], F32)


    @block.sync
    def _(sync):
        ctx.inc(sync.dma_start(sg11[:, 0:1], etg[:, :]), "ld0", 16)
        ctx.inc(sync.dma_start(sg11[:, 1:2], isg[:, :]), "ld0", 16)

    @block.scalar
    def _(s):
        ctx.wait(s, "ld0")
        ctx.inc(s.activation(sg11[:, 0:1], sg11[:, 0:1], AF.Sigmoid), "sg", 1)

    @block.vector
    def _(v):
        ctx.inc(v.memset(ones_r[:, :], 1.0), "prep", 1)

    @block.tensor
    def _(t):
        ctx.wait(t, "sg", 1)
        ctx.wait(t, "prep", 1)
        ctx.inc(t.matmul(ps_b[:, :], ones_r[:, :], sg11[:, :], start=True, stop=True), "bc", 1)

    @block.vector
    def _(v):
        ctx.wait(v, "bc", 1)
        ctx.inc(v.tensor_copy(sgig[:, :], ps_b[:, :]), "bcv", 1)

    Hglob = [0]
    NQC = _BLIM.get("nqc", 4)
    for b in range(_BLIM.get("nb", BPC)):
        @block.sync
        def _(sync, b=b):
            if b > 0:
                ctx.wait(sync, "cst", 16 * 256 * b)
            for c in range(NC):
                ctx.inc(sync.dma_start(qt_sb[c][:, :], qtf[b, c, :, :]), f"lq{b}", 16)
            for c in range(NC):
                ctx.inc(sync.dma_start(kt_sb[c][:, :], ktf[b, c, :, :]), f"lq{b}", 16)
            for tt in range(NT):
                ctx.inc(sync.dma_start(va_sb[tt][:, :], vaug[b, tt, :, :]), f"lq{b}", 16)

        for qc in range(NQC):
            g = b * NQC + qc
            @block.sync
            def _(sync, b=b, qc=qc, g=g):
                for kc in range(16):
                    n = g * 16 + kc
                    if n >= 2:
                        ctx.wait(sync, "etv", n - 1)
                    ctx.inc(sync.dma_start(tp_f[kc % 2][:, :],
                                           tpT[kc * 128:(kc + 1) * 128, qc * 512:(qc + 1) * 512]), "ltp", 16)

            @block.vector
            def _(v, g=g):
                if g == 0:
                    ctx.wait(v, "bcv", 1)
                for kc in range(16):
                    n = g * 16 + kc
                    ctx.wait(v, "ltp", 16 * (n + 1))
                    if n >= 2:
                        ctx.wait(v, "ete", n - 1)
                    if g > 0 and kc == 0:
                        ctx.wait(v, "pmul", 16 * 16 * g)
                    ctx.inc(v.tensor_scalar(tp2_f[kc % 2][:, :], tp_f[kc % 2][:, :],
                                            sgig[:, 0:1], sgig[:, 1:2], ALU.mult, ALU.add), "etv", 1)

            @block.scalar
            def _(s, g=g):
                for kc in range(16):
                    n = g * 16 + kc
                    ctx.wait(s, "etv", n + 1)
                    ctx.inc(s.activation(etp[kc][:, :], tp2_f[kc % 2][:, :], AF.Exp), "ete", 1)

            for h in range(_BLIM.get("nh", NH)):
                H = Hglob[0]
                Hglob[0] += 1
                hb = H % 2
                ch = h // 2
                p0 = (h % 2) * 64

                @block.tensor
                def _(t, b=b, qc=qc, h=h, H=H, hb=hb, ch=ch, p0=p0, g=g):
                    if h == 0:
                        ctx.wait(t, f"lq{b}", 16 * 16)
                    for kc in range(16):
                        n = H * 16 + kc
                        if n >= 2:
                            ctx.wait(t, "sexp", n - 1)
                        ctx.inc(t.matmul(ps_s[n % 2][:, :],
                                         kt_sb[ch][p0:p0 + 64, kc * 128:(kc + 1) * 128],
                                         qt_sb[ch][p0:p0 + 64, qc * 512:(qc + 1) * 512],
                                         start=True, stop=True), "smm", 1)

                @block.scalar
                def _(s, H=H, g=g):
                    for kc in range(16):
                        n = H * 16 + kc
                        ctx.wait(s, "smm", n + 1)
                        if n >= 2:
                            ctx.wait(s, "pmul", n - 1)
                        ctx.inc(s.activation(e_sb[n % 2][:, :], ps_s[n % 2][:, :], AF.Exp), "sexp", 1)

                @block.vector
                def _(v, H=H, hb=hb, g=g):
                    for kc in range(16):
                        n = H * 16 + kc
                        ctx.wait(v, "sexp", n + 1)
                        ctx.wait(v, "ete", g * 16 + kc + 1)
                        if H >= 2:
                            ctx.wait(v, "avm", 4 * (H - 1))
                        ctx.inc(v.tensor_tensor(pt[hb][kc][:, :], e_sb[n % 2][:, :],
                                                etp[kc][:, :], ALU.mult), "pmul", 1)

                if not _BLIM.get("av", True):
                    continue
                @block.tensor
                def _(t, b=b, h=h, H=H, hb=hb):
                    if h == 0:
                        ctx.wait(t, f"lq{b}", 16 * 32)
                    for qs in range(4):
                        m = H * 4 + qs
                        ctx.wait(t, "pmul", 16 * (H + 1))
                        if m >= 2:
                            ctx.wait(t, "cev", m - 1)
                        for kc in range(16):
                            mi = t.matmul(ps_c[m % 2][:, :],
                                          pt[hb][kc][:, qs * 128:(qs + 1) * 128],
                                          va_sb[kc][:, h * 65:h * 65 + 65],
                                          start=(kc == 0), stop=(kc == 15))
                            if kc == 15:
                                ctx.inc(mi, "avm", 1)

                @block.vector
                def _(v, H=H):
                    mode = _BLIM.get("evac", "full")
                    for qs in range(4):
                        m = H * 4 + qs
                        ctx.wait(v, "avm", m + 1)
                        if m >= 2:
                            ctx.wait(v, "cst", 16 * (m - 1))
                        if mode == "copy":
                            ctx.inc(v.tensor_copy(ctx_sb[m % 2][:, :], ps_c[m % 2][:, 0:64]), "cev", 1)
                            continue
                        ctx.inc(v.reciprocal(rc[:, m % 2:m % 2 + 1], ps_c[m % 2][:, 64:65]), "rcp", 1)
                        ctx.wait(v, "rcp", m + 1)
                        if mode == "recip":
                            ctx.inc(v.tensor_copy(ctx_sb[m % 2][:, :], ps_c[m % 2][:, 0:64]), "cev", 1)
                            continue
                        ctx.inc(v.tensor_scalar(ctx_sb[m % 2][:, :], ps_c[m % 2][:, 0:64],
                                                rc[:, m % 2:m % 2 + 1], None, ALU.mult), "cev", 1)

                @block.gpsimd
                def _(g_, b=b, h=h, qc=qc, H=H):
                    for qs in range(4):
                        m = H * 4 + qs
                        ctx.wait(g_, "cev", m + 1)
                        q0 = qc * 512 + qs * 128
                        ctx.inc(g_.dma_start(ctxo[b, h, q0:q0 + 128, :], ctx_sb[m % 2][:, :]), "cst", 16)

    @block.gpsimd
    def _(g_):
        ctx.wait(g_, "cst")
        ctx.wait(g_, "pmul")



def build_B():
    nc = bass.Bass("TRN2", target_bir_lowering=False)
    es = contextlib.ExitStack()
    d = {}
    decl_B(nc, d)
    with nc.Block() as block:
        ctx = Ctx(nc, es)
        ctx.ncblock = block
        emit_B(nc, ctx, block, d)
    return nc, es


def decl_C(nc, d):
    if "xn0fm" not in d:
        d["xn0fm"] = nc.dram_tensor("xn0fm", [BPC, NC, 128, T], BF16, kind="ExternalInput")
    if "xin" not in d:
        d["xin"] = nc.dram_tensor("xin", [BPC, T, D], F32, kind="ExternalInput")
    if "router_w" not in d:
        d["router_w"] = nc.dram_tensor("router_w", [D, 8], F32, kind="ExternalInput")
    if "w_gate" not in d:
        d["w_gate"] = nc.dram_tensor("w_gate", [8, D, 512], F32, kind="ExternalInput")
    if "w_up" not in d:
        d["w_up"] = nc.dram_tensor("w_up", [8, D, 512], F32, kind="ExternalInput")
    if "w_down" not in d:
        d["w_down"] = nc.dram_tensor("w_down", [8, 512, D], F32, kind="ExternalInput")
    if "ws_gate" not in d:
        d["ws_gate"] = nc.dram_tensor("ws_gate", [D, D], F32, kind="ExternalInput")
    if "ws_up" not in d:
        d["ws_up"] = nc.dram_tensor("ws_up", [D, D], F32, kind="ExternalInput")
    if "ws_down" not in d:
        d["ws_down"] = nc.dram_tensor("ws_down", [D, D], F32, kind="ExternalInput")
    if "ident_bf" not in d:
        d["ident_bf"] = nc.dram_tensor("ident_bf", [128, 128], BF16, kind="ExternalInput")
    if "x1p" not in d:
        d["x1p"] = nc.dram_tensor("x1p", [BPC, NT, 128, D], F32, kind="ExternalOutput")


def emit_C(nc, ctx, block, d):
    xn0fm = d["xn0fm"]
    xin = d["xin"]
    rw = d["router_w"]
    wg = d["w_gate"]
    wu = d["w_up"]
    wd = d["w_down"]
    wsg = d["ws_gate"]
    wsu = d["ws_up"]
    wsd = d["ws_down"]
    ident = d["ident_bf"]
    x1p = d["x1p"]

    NE = _CLIM.get("ne", 10)
    NGRP = _CLIM.get("ngrp", BPC)

    sb = ctx.sb
    xnfm = [sb(f"cxn{c}", [128, T], BF16) for c in range(NC)]
    id_s = sb("cid", [128, 128], BF16)
    rw_sb = [sb(f"crw{c}", [128, 8], BF16) for c in range(NC)]
    wstg = [sb(f"cwstg{i}", [128, D], F32) for i in range(2)]
    wg_sb = [[sb(f"cwg{i}_{c}", [128, 512], BF16) for c in range(NC)] for i in range(2)]
    wu_sb = [[sb(f"cwu{i}_{c}", [128, 512], BF16) for c in range(NC)] for i in range(2)]
    wd_sb = [[sb(f"cwd{i}_{f}", [128, D], BF16) for f in range(4)] for i in range(2)]
    ex8 = [sb(f"cex8{i}", [128, 8], F32) for i in range(2)]
    rtmp = [sb(f"crt{i}", [128, 8], F32) for i in range(2)]
    rcol = [sb(f"crc{i}", [128, 4], F32) for i in range(2)]
    c_all = sb("call", [128, NT * 8], F32)
    sgt = [sb(f"csg{i}", [128, 512], BF16) for i in range(2)]
    upt = [sb(f"cup{i}", [128, 512], BF16) for i in range(2)]
    ht = [sb(f"cht{i}", [128, 512], BF16) for i in range(2)]
    htr = [sb(f"chtr{i}", [128, 512], BF16) for i in range(2)]
    y_sb = [sb(f"cy{t}", [128, D], F32) for t in range(NT)]
    xs = [sb(f"cxs{i}", [128, D], F32) for i in range(2)]
    o_t = [sb(f"cot{i}", [128, D], F32) for i in range(2)]

    ps_g = ctx.ps("cpg", [128, 512], F32)
    ps_u = ctx.ps("cpu", [128, 512], F32)
    ps_tr = [ctx.ps(f"cptr{i}", [128, 1024], BF16) for i in range(2)]
    ps_y = [ctx.ps(f"cpy{i}", [128, D], F32) for i in range(2)]

    def exp_chunks(e):
        if e < 8:
            g = [wg[e, c * 128:(c + 1) * 128, :] for c in range(NC)]
            u = [wu[e, c * 128:(c + 1) * 128, :] for c in range(NC)]
            d = [wd[e, f * 128:(f + 1) * 128, :] for f in range(4)]
        else:
            hf = e - 8
            g = [wsg[c * 128:(c + 1) * 128, hf * 512:(hf + 1) * 512] for c in range(NC)]
            u = [wsu[c * 128:(c + 1) * 128, hf * 512:(hf + 1) * 512] for c in range(NC)]
            d = [wsd[hf * 512 + f * 128:hf * 512 + (f + 1) * 128, :] for f in range(4)]
        return g, u, d


    @block.sync
    def _(sync):
        ctx.inc(sync.dma_start(id_s[:, :], ident[:, :]), "ld0", 16)
        ctx.inc(sync.dma_start(wstg[0][:, 0:8 * NC].rearrange("p (c e) -> p c e", c=NC), rw[:, :].rearrange("(c p) e -> p c e", p=128)), "ld0", 16)

    @block.vector
    def _(v):
        ctx.wait(v, "ld0")
        for c in range(NC):
            ctx.inc(v.tensor_copy(rw_sb[c][:, :], wstg[0][:, c * 8:(c + 1) * 8]), "rwc", 1)

    for grp in range(NGRP):
        b = grp
        gi = grp

        @block.sync
        def _(sync, b=b, gi=gi):
            if gi > 0:
                ctx.wait(sync, "dwn", gi * NE * NT)
                ctx.wait(sync, "rmm", gi * NT)
            for c in range(NC):
                ctx.inc(sync.dma_start(xnfm[c][:, :], xn0fm[b, c, :, :]), f"lxn{gi}", 16)

        @block.tensor
        def _(t, gi=gi):
            ctx.wait(t, f"lxn{gi}", 16 * NC)
            ctx.wait(t, "rwc", NC)
            if gi > 0:
                ctx.wait(t, "sil", gi * NE * NT)
                ctx.wait(t, "upc", gi * NE * NT)
            for tt in range(NT):
                n = gi * NT + tt
                if n >= 2:
                    ctx.wait(t, "rex", n - 1)
                pr = (ps_g if tt % 2 == 0 else ps_u)
                for c in range(NC):
                    mi = t.matmul(pr[:, 0:8], xnfm[c][:, tt * 128:(tt + 1) * 128],
                                  rw_sb[c][:, :], start=(c == 0), stop=(c == NC - 1))
                ctx.inc(mi, "rmm", 1)

        @block.scalar
        def _(s, gi=gi):
            for tt in range(NT):
                n = gi * NT + tt
                ctx.wait(s, "rmm", n + 1)
                if n >= 2:
                    ctx.wait(s, "rsf", 9 * (n - 1))
                pr = (ps_g if tt % 2 == 0 else ps_u)
                ctx.inc(s.activation(ex8[tt % 2][:, :], pr[:, 0:8], AF.Exp), "rex", 1)

        @block.vector
        def _(v, gi=gi):
            for tt in range(NT):
                n = gi * NT + tt
                i = tt % 2
                ctx.wait(v, "rex", n + 1)
                def step(ins):
                    ctx.inc(ins, "rsf", 1)
                    ctx.wait(v, "rsf", ctx.v("rsf"))
                step(v.reduce_sum(rcol[i][:, 0:1], ex8[i][:, :], axis=AX.X))
                step(v.reciprocal(rcol[i][:, 1:2], rcol[i][:, 0:1]))
                step(v.tensor_scalar(rtmp[i][:, :], ex8[i][:, :], rcol[i][:, 1:2], None, ALU.mult))
                step(v.tensor_reduce(rcol[i][:, 2:3], rtmp[i][:, :], AX.X, ALU.min))
                step(v.tensor_scalar(ex8[i][:, :], rtmp[i][:, :], rcol[i][:, 2:3], 99.0, ALU.is_equal, ALU.mult))
                step(v.tensor_tensor(ex8[i][:, :], rtmp[i][:, :], ex8[i][:, :], ALU.add))
                step(v.tensor_reduce(rcol[i][:, 3:4], ex8[i][:, :], AX.X, ALU.min))
                step(v.tensor_scalar(ex8[i][:, :], rtmp[i][:, :], rcol[i][:, 3:4], None, ALU.is_gt))
                step(v.tensor_tensor(c_all[:, tt * 8:(tt + 1) * 8], rtmp[i][:, :], ex8[i][:, :], ALU.mult))

        for e in range(NE):
            eb = (gi * NE + e) % 2
            gch, uch, dch = exp_chunks(e)
            plan = [(gch[c], wg_sb[eb][c]) for c in range(NC)] +                        [(uch[c], wu_sb[eb][c]) for c in range(NC)] +                        [(dch[f], wd_sb[eb][f]) for f in range(4)]
            ldtag = f"wl{gi}_{e}"
            wctag = f"wc{gi}_{e}"

            @block.sync
            def _(sync, plan=plan, ldtag=ldtag, wctag=wctag, gi=gi, e=e):
                for j, (src, dst) in enumerate(plan):
                    if j >= 2:
                        ctx.wait(sync, wctag, j - 1)
                    elif gi + e > 0:
                        pe_, pg = (e - 1, gi) if e > 0 else (NE - 1, gi - 1)
                        ctx.wait(sync, f"wc{pg}_{pe_}", 20)
                    else:
                        ctx.wait(sync, "rwc", NC)
                    ctx.inc(sync.dma_start(wstg[j % 2][:, 0:src.shape[1]], src), ldtag, 16)

            @block.vector
            def _(v, plan=plan, ldtag=ldtag, wctag=wctag, gi=gi, e=e):
                me_prev = (gi * NE + e - 2)
                if me_prev >= 0:
                    ctx.wait(v, "dwn", (me_prev + 1) * NT)
                for j, (src, dst) in enumerate(plan):
                    ctx.wait(v, ldtag, 16 * (j + 1))
                    ctx.inc(v.tensor_copy(dst[:, 0:src.shape[1]], wstg[j % 2][:, 0:src.shape[1]]), wctag, 1)

            for tt in range(NT):
                m = (gi * NE + e) * NT + tt
                ml = e * NT + tt

                @block.tensor
                def _(t, m=m, tt=tt, e=e, eb=eb, gi=gi, wctag=wctag):
                    if tt == 0:
                        ctx.wait(t, wctag, 20)
                    if tt == 0 and e == 0:
                        ctx.wait(t, "rex", (gi + 1) * NT)
                    ctx.wait(t, "sil", m)
                    for c in range(NC):
                        mi = t.matmul(ps_g[:, :], xnfm[c][:, tt * 128:(tt + 1) * 128],
                                      wg_sb[eb][c][:, :], start=(c == 0), stop=(c == NC - 1))
                    ctx.inc(mi, "gmm", 1)
                    ctx.wait(t, "upc", m)
                    for c in range(NC):
                        mi = t.matmul(ps_u[:, :], xnfm[c][:, tt * 128:(tt + 1) * 128],
                                      wu_sb[eb][c][:, :], start=(c == 0), stop=(c == NC - 1))
                    ctx.inc(mi, "umm", 1)
                    if m - 1 >= gi * NE * NT:
                        pm = m - 1
                        ctx.wait(t, "htc", pm + 1)
                        if pm >= 2:
                            ctx.wait(t, "tre", pm - 1)
                        for f in range(4):
                            mi = t.transpose(ps_tr[pm % 2][:, f * 128:(f + 1) * 128],
                                             ht[pm % 2][:, f * 128:(f + 1) * 128], id_s[:, :])
                        ctx.inc(mi, "trd", 1)
                        ctx.wait(t, "tre", pm + 1)
                        if pm >= 2:
                            ctx.wait(t, "yac", pm - 1)
                        pe_, ptt = divmod(pm, NT)
                        peb = pe_ % 2
                        for f in range(4):
                            for hN in range(2):
                                mi = t.matmul(ps_y[pm % 2][:, hN * 512:(hN + 1) * 512],
                                              htr[pm % 2][:, f * 128:(f + 1) * 128],
                                              wd_sb[peb][f][:, hN * 512:(hN + 1) * 512],
                                              start=(f == 0), stop=(f == 3))
                        ctx.inc(mi, "dwn", 1)

                @block.scalar
                def _(s, m=m):
                    ctx.wait(s, "gmm", m + 1)
                    if m >= 2:
                        ctx.wait(s, "htc", m - 1)
                    ctx.inc(s.activation(sgt[m % 2][:, :], ps_g[:, :], AF.Silu), "sil", 1)
                    if m - 1 >= (m // (NE * NT)) * NE * NT and m >= 1:
                        pm = m - 1
                        ctx.wait(s, "trd", pm + 1)
                        if pm >= 2:
                            ctx.wait(s, "dwn", pm - 1)
                        ctx.inc(s.activation(htr[pm % 2][:, :], ps_tr[pm % 2][:, 0:512], AF.Copy), "tre", 1)

                @block.vector
                def _(v, m=m, tt=tt, e=e, gi=gi):
                    ctx.wait(v, "umm", m + 1)
                    if m >= 2:
                        ctx.wait(v, "htc", m - 1)
                    if e < 8:
                        ctx.inc(v.tensor_scalar(upt[m % 2][:, :], ps_u[:, :],
                                                c_all[:, tt * 8 + e:tt * 8 + e + 1], None, ALU.mult), "upc", 1)
                    else:
                        ctx.inc(v.tensor_copy(upt[m % 2][:, :], ps_u[:, :]), "upc", 1)
                    ctx.wait(v, "sil", m + 1)
                    if m >= 2:
                        ctx.wait(v, "trd", m - 1)
                    ctx.inc(v.tensor_tensor(ht[m % 2][:, :], sgt[m % 2][:, :], upt[m % 2][:, :], ALU.mult), "htc", 1)
                    if m - 1 >= gi * NE * NT:
                        pm = m - 1
                        pe_, ptt = divmod(pm, NT)
                        pe_ %= NE
                        ctx.wait(v, "dwn", pm + 1)
                        if pe_ == 0:
                            ctx.inc(v.tensor_copy(y_sb[ptt][:, :], ps_y[pm % 2][:, :]), "yac", 1)
                        else:
                            ctx.inc(v.tensor_tensor(y_sb[ptt][:, :], y_sb[ptt][:, :], ps_y[pm % 2][:, :], ALU.add), "yac", 1)

        m = (gi * NE + NE - 1) * NT + NT - 1 + 1

        @block.tensor
        def _(t, m=m, gi=gi):
            pm = m - 1
            ctx.wait(t, "htc", pm + 1)
            ctx.wait(t, "tre", pm - 1)
            for f in range(4):
                mi = t.transpose(ps_tr[pm % 2][:, f * 128:(f + 1) * 128],
                                 ht[pm % 2][:, f * 128:(f + 1) * 128], id_s[:, :])
            ctx.inc(mi, "trd", 1)
            ctx.wait(t, "tre", pm + 1)
            ctx.wait(t, "yac", pm - 1)
            peb = (gi * NE + NE - 1) % 2
            for f in range(4):
                for hN in range(2):
                    mi = t.matmul(ps_y[pm % 2][:, hN * 512:(hN + 1) * 512],
                                  htr[pm % 2][:, f * 128:(f + 1) * 128],
                                  wd_sb[peb][f][:, hN * 512:(hN + 1) * 512],
                                  start=(f == 0), stop=(f == 3))
            ctx.inc(mi, "dwn", 1)

        @block.scalar
        def _(s, m=m):
            pm = m - 1
            ctx.wait(s, "trd", pm + 1)
            ctx.wait(s, "dwn", pm - 1)
            ctx.inc(s.activation(htr[pm % 2][:, :], ps_tr[pm % 2][:, 0:512], AF.Copy), "tre", 1)

        @block.vector
        def _(v, m=m, gi=gi):
            pm = m - 1
            pe_, ptt = divmod(pm, NT)
            pe_ %= NE
            ctx.wait(v, "dwn", pm + 1)
            if pe_ == 0:
                ctx.inc(v.tensor_copy(y_sb[ptt][:, :], ps_y[pm % 2][:, :]), "yac", 1)
            else:
                ctx.inc(v.tensor_tensor(y_sb[ptt][:, :], y_sb[ptt][:, :], ps_y[pm % 2][:, :], ALU.add), "yac", 1)

        @block.sync
        def _(sync, b=b, gi=gi):
            for tt in range(NT):
                n = gi * NT + tt
                if n >= 2:
                    ctx.wait(sync, "xad", n - 1)
                ctx.inc(sync.dma_start(xs[tt % 2][:, :], xin[b, tt * 128:(tt + 1) * 128, :]), "lx", 16)

        @block.vector
        def _(v, gi=gi):
            for tt in range(NT):
                n = gi * NT + tt
                ctx.wait(v, "lx", 16 * (n + 1))
                ctx.wait(v, "yac", gi * NE * NT + (NE - 1) * NT + tt + 1)
                if n >= 2:
                    ctx.wait(v, "ost", 16 * (n - 1))
                ctx.inc(v.tensor_tensor(o_t[tt % 2][:, :], xs[tt % 2][:, :], y_sb[tt][:, :], ALU.add), "xad", 1)

        @block.gpsimd
        def _(g_, b=b, gi=gi):
            for tt in range(NT):
                n = gi * NT + tt
                ctx.wait(g_, "xad", n + 1)
                ctx.inc(g_.dma_start(x1p[b, tt, :, :], o_t[tt % 2][:, :]), "ost", 16)

    @block.gpsimd
    def _(g_):
        ctx.wait(g_, "ost")



def build_C():
    nc = bass.Bass("TRN2", target_bir_lowering=False)
    es = contextlib.ExitStack()
    d = {}
    decl_C(nc, d)
    with nc.Block() as block:
        ctx = Ctx(nc, es)
        ctx.ncblock = block
        emit_C(nc, ctx, block, d)
    return nc, es


def decl_D(nc, d):
    if "yscr" not in d:
        d["yscr"] = nc.dram_tensor("yscr", [BPC, T, D], BF16, kind="ExternalInput")
    if "x1p" not in d:
        d["x1p"] = nc.dram_tensor("x1p", [BPC, NT, 128, D], F32, kind="ExternalInput")
    if "woT" not in d:
        d["woT"] = nc.dram_tensor("woT", [D, D], F32, kind="ExternalInput")
    if "bo_r" not in d:
        d["bo_r"] = nc.dram_tensor("bo_r", [1, D], F32, kind="ExternalInput")
    if "ident_bf" not in d:
        d["ident_bf"] = nc.dram_tensor("ident_bf", [128, 128], BF16, kind="ExternalInput")
    if "x1" not in d:
        d["x1"] = nc.dram_tensor("x1", [BPC, NT, 128, D], F32, kind="ExternalOutput")
    if "xn1fm" not in d:
        d["xn1fm"] = nc.dram_tensor("xn1fm", [BPC, NC, 128, T + 2], BF16, kind="ExternalOutput")


def emit_D(nc, ctx, block, d):
    yscr = d["yscr"]
    x1p = d["x1p"]
    woT = d["woT"]
    bo_r = d["bo_r"]
    ident = d["ident_bf"]
    x1 = d["x1"]
    xn1fm = d["xn1fm"]

    sb = ctx.sb
    ysb = [sb(f"dy{i}", [128, D], BF16) for i in range(2)]
    ctr = [sb(f"dc{i}", [128, D], BF16) for i in range(2)]
    w_sb = [sb(f"dw{c}", [128, D], BF16) for c in range(NC)]
    wstg = [sb(f"dwst{i}", [128, D], F32) for i in range(2)]
    xp = [sb(f"dxp{i}", [128, D], F32) for i in range(2)]
    x1o = [sb(f"dx1{i}", [128, D], F32) for i in range(2)]
    bor_s = sb("dbor", [1, D], F32)
    ones_r = sb("dones", [1, 128], F32)
    id_s = sb("did", [128, 128], BF16)
    zcol = sb("dz", [128, 1], BF16)
    epscol = sb("deps", [128, 1], F32)
    x_t = [sb(f"dlx{i}", [128, D], F32) for i in range(2)]
    sq_t = [sb(f"dlsq{i}", [128, D], F32) for i in range(2)]
    st = [sb(f"dlst{i}", [128, 8], F32) for i in range(2)]
    xnb = [sb(f"dlxnb{i}", [128, D], BF16) for i in range(2)]
    xnfm = [sb(f"dxnfm{c}", [128, T + 2], BF16) for c in range(NC)]

    ps_tr = [ctx.ps(f"dptr{i}", [128, 1024], BF16) for i in range(2)]
    ps_w = [ctx.ps(f"dpw{i}", [128, D], F32) for i in range(2)]


    @block.sync
    def _(sync):
        ctx.inc(sync.dma_start(id_s[:, :], ident[:, :]), "ld0", 16)
        ctx.inc(sync.dma_start(bor_s[:, :], bo_r[:, :]), "ld0", 16)

    @block.vector
    def _(v):
        ctx.wait(v, "ld0")
        v.memset(ones_r[:, :], 1.0)
        v.memset(zcol[:, :], 0.0)
        for c in range(NC):
            v.memset(xnfm[c][:, 0:1], 0.0)
            v.memset(xnfm[c][:, T + 1:T + 2], 0.0)
        ctx.inc(v.memset(epscol[:, :], EPS_LN), "prep", 1)

    chunks = [woT[c * 128:(c + 1) * 128, :] for c in range(NC)]
    emit_weight_load(ctx, block, "dwo", chunks, w_sb, wstg)

    @block.sync
    def _(sync):
        for b in range(BPC):
            for tt in range(NT):
                n = b * NT + tt
                if n >= 2:
                    ctx.wait(sync, "trd", 2 * (n - 1))
                ctx.inc(sync.dma_start(ysb[n % 2][:, :], yscr[b, tt * 128:(tt + 1) * 128, :]), "ly", 16)
                if n >= 2:
                    ctx.wait(sync, "xad", n - 1)
                ctx.inc(sync.dma_start(xp[n % 2][:, :], x1p[b, tt, :, :]), "lp", 16)

    @block.tensor
    def _(t):
        ctx.wait(t, "ld0")
        ctx.wait(t, "dwowc", NC)
        for b in range(BPC):
            for tt in range(NT):
                n = b * NT + tt
                ctx.wait(t, "ly", 16 * (n + 1))
                for grp4 in range(2):
                    gidx = n * 2 + grp4
                    if gidx >= 2:
                        ctx.wait(t, "tre", gidx - 1)
                    for f in range(4):
                        c = grp4 * 4 + f
                        mi = t.transpose(ps_tr[gidx % 2][:, f * 128:(f + 1) * 128],
                                         ysb[n % 2][:, c * 128:(c + 1) * 128], id_s[:, :])
                    ctx.inc(mi, "trd", 1)
                ctx.wait(t, "tre", 2 * (n + 1))
                if n >= 2:
                    ctx.wait(t, "xad", n - 1)
                for hN in range(2):
                    for c in range(NC):
                        mi = t.matmul(ps_w[n % 2][:, hN * 512:(hN + 1) * 512],
                                      ctr[n % 2][:, c * 128:(c + 1) * 128],
                                      w_sb[c][:, hN * 512:(hN + 1) * 512],
                                      start=(c == 0), stop=False)
                    mi = t.matmul(ps_w[n % 2][:, hN * 512:(hN + 1) * 512], ones_r[:, :],
                                  bor_s[:, hN * 512:(hN + 1) * 512], start=False, stop=True)
                ctx.inc(mi, "wmm", 1)

    @block.scalar
    def _(s):
        for b in range(BPC):
            for tt in range(NT):
                n = b * NT + tt
                for grp4 in range(2):
                    gidx = n * 2 + grp4
                    ctx.wait(s, "trd", gidx + 1)
                    if gidx >= 2:
                        ctx.wait(s, "wmm", (gidx - 2) // 2 + 1)
                    ctx.inc(s.activation(ctr[n % 2][:, grp4 * 512:(grp4 + 1) * 512],
                                         ps_tr[gidx % 2][:, 0:512], AF.Copy), "tre", 1)

    @block.vector
    def _(v):
        for b in range(BPC):
            for tt in range(NT):
                n = b * NT + tt
                ctx.wait(v, "wmm", n + 1)
                ctx.wait(v, "lp", 16 * (n + 1))
                if n >= 2:
                    ctx.wait(v, "x1s", 16 * (n - 1))
                ctx.inc(v.tensor_tensor(x1o[n % 2][:, :], xp[n % 2][:, :], ps_w[n % 2][:, :], ALU.add), "xad", 1)

    @block.gpsimd
    def _(g_):
        for b in range(BPC):
            for tt in range(NT):
                n = b * NT + tt
                ctx.wait(g_, "xad", n + 1)
                ctx.inc(g_.dma_start(x1[b, tt, :, :], x1o[n % 2][:, :]), "x1s", 16)

    tmp = {"x": x_t, "sq": sq_t, "st": st, "xnb": xnb, "ps_trb": ps_tr, "epscol": epscol}
    for b in range(BPC):
        def srcd(sync, tt, buf, b=b):
            n = b * NT + tt
            ctx.wait(sync, "x1s", 16 * (n + 1))
            return sync.dma_start(buf[:, :], x1[b, tt, :, :])

        def pe_wait(t, b=b):
            if b == 0:
                ctx.wait(t, "wmm", BPC * NT)
            ctx.wait(t, "tre", 2 * BPC * NT)

        emit_ln_to_fm(ctx, block, f"D{b}", NT, srcd, lambda c, b=b: xnfm[c][:, 1:T + 1], tmp, id_s,
                      extra_pe_wait=pe_wait, prev_tag=(f"D{b-1}" if b > 0 else None))

        @block.gpsimd
        def _(g_, b=b):
            ctx.wait(g_, f"D{b}trev", NT * NC)
            ctx.wait(g_, "prep", 1)
            for c in range(NC):
                ctx.inc(g_.dma_start(xn1fm[b, c, :, :], xnfm[c][:, :]), f"fs{b}", 16)

    @block.gpsimd
    def _(g_):
        ctx.wait(g_, "x1s")
        ctx.wait(g_, "fs0")
        ctx.wait(g_, "fs1")



def build_D():
    nc = bass.Bass("TRN2", target_bir_lowering=False)
    es = contextlib.ExitStack()
    d = {}
    decl_D(nc, d)
    with nc.Block() as block:
        ctx = Ctx(nc, es)
        ctx.ncblock = block
        emit_D(nc, ctx, block, d)
    return nc, es


def decl_E(nc, d):
    if "xn1fm" not in d:
        d["xn1fm"] = nc.dram_tensor("xn1fm", [BPC, NC, 128, T + 2], BF16, kind="ExternalInput")
    if "wrT2" not in d:
        d["wrT2"] = nc.dram_tensor("wrT2", [D, D], F32, kind="ExternalInput")
    if "wkT2" not in d:
        d["wkT2"] = nc.dram_tensor("wkT2", [D, D], F32, kind="ExternalInput")
    if "wvT2" not in d:
        d["wvT2"] = nc.dram_tensor("wvT2", [D, D], F32, kind="ExternalInput")
    if "woT2" not in d:
        d["woT2"] = nc.dram_tensor("woT2", [D, D], F32, kind="ExternalInput")
    if "br_c2" not in d:
        d["br_c2"] = nc.dram_tensor("br_c2", [128, NC], F32, kind="ExternalInput")
    if "bk_c2" not in d:
        d["bk_c2"] = nc.dram_tensor("bk_c2", [128, NC], F32, kind="ExternalInput")
    if "bv_c2" not in d:
        d["bv_c2"] = nc.dram_tensor("bv_c2", [128, NC], F32, kind="ExternalInput")
    if "bo_r2" not in d:
        d["bo_r2"] = nc.dram_tensor("bo_r2", [1, D], F32, kind="ExternalInput")
    if "td_c" not in d:
        d["td_c"] = nc.dram_tensor("td_c", [128, NC], F32, kind="ExternalInput")
    if "u_c" not in d:
        d["u_c"] = nc.dram_tensor("u_c", [128, NC], F32, kind="ExternalInput")
    if "trem_row" not in d:
        d["trem_row"] = nc.dram_tensor("trem_row", [1, T], F32, kind="ExternalInput")
    if "att1" not in d:
        d["att1"] = nc.dram_tensor("att1", [BPC, NT, 128, D], F32, kind="ExternalOutput")
    if "wkvd" not in d:
        d["wkvd"] = nc.dram_tensor("wkvd", [BPC, NC, 128, T], F32, kind="Internal")


def emit_E(nc, ctx, block, d):
    xn1fm = d["xn1fm"]
    wrT2 = d["wrT2"]
    wkT2 = d["wkT2"]
    wvT2 = d["wvT2"]
    woT2 = d["woT2"]
    br_c2 = d["br_c2"]
    bk_c2 = d["bk_c2"]
    bv_c2 = d["bv_c2"]
    bo_r2 = d["bo_r2"]
    td_c = d["td_c"]
    u_c = d["u_c"]
    trem_row = d["trem_row"]
    att1 = d["att1"]
    wkvd = d["wkvd"]

    sb = ctx.sb
    xn1 = [sb(f"exn{c}", [128, T + 2], BF16) for c in range(NC)]
    wA = [sb(f"ewA{c}", [128, D], BF16) for c in range(NC)]
    wB = [sb(f"ewB{c}", [128, D], BF16) for c in range(NC)]
    wstg = [sb(f"ewst{i}", [128, D], F32) for i in range(2)]
    outpre = sb("eop", [128, NC * T], BF16)
    zk = sb("ezk", [128, T + 2], BF16)
    dkv = sb("edkv", [128, T], BF16)
    t1b = sb("et1", [128, T], BF16)
    vfin = sb("evf", [128, T], F32)
    e_b = sb("eeb", [128, T], F32)
    eu_b = sb("eub", [128, T], F32)
    ev_b = sb("eevb", [128, T], F32)
    sa = sb("esa", [128, T + 1], F32)
    sbf = sb("esb", [128, T + 1], F32)
    ga = sb("ega", [128, T], F32)
    gb = sb("egb", [128, T], F32)
    ntr = sb("entr", [128, T], F32)
    ones2k = sb("eon2", [128, T], BF16)
    cols = sb("ecols", [128, 8 * NC], F32)
    bor_s = sb("ebor", [1, D], F32)
    ones_r = sb("eonr", [1, 128], F32)
    att_t = [sb(f"eat{i}", [128, D], F32) for i in range(2)]

    ps_z = [ctx.ps(f"epz{i}", [128, 512], F32) for i in range(2)]
    ps_a = [ctx.ps(f"epa{i}", [128, D], F32) for i in range(2)]

    COL_BR, COL_BK, COL_BV, COL_W, COL_U, COL_BKU = 0, 1, 2, 3, 4, 5

    def col(kind, d):
        return cols[:, kind * NC + d:kind * NC + d + 1]

    ctx.lastcast = None

    @block.sync
    def _(sync):
        ctx.inc(sync.dma_start(cols[:, COL_BR * NC:COL_BR * NC + NC], br_c2[:, :]), "ld0", 16)
        ctx.inc(sync.dma_start(cols[:, COL_BK * NC:COL_BK * NC + NC], bk_c2[:, :]), "ld0", 16)
        ctx.inc(sync.dma_start(cols[:, COL_BV * NC:COL_BV * NC + NC], bv_c2[:, :]), "ld0", 16)
        ctx.inc(sync.dma_start(cols[:, COL_W * NC:COL_W * NC + NC], td_c[:, :]), "ld0", 16)
        ctx.inc(sync.dma_start(cols[:, COL_U * NC:COL_U * NC + NC], u_c[:, :]), "ld0", 16)
        ctx.inc(sync.dma_start(bor_s[:, :], bo_r2[:, :]), "ld0", 16)
        ctx.inc(sync.dma_start(ntr[:, :], trem_row[:, :].partition_broadcast(128).squeeze(1)), "ld0", 16)

    @block.vector
    def _(v):
        ctx.wait(v, "ld0")
        v.memset(ones2k[:, :], 1.0)
        v.memset(ones_r[:, :], 1.0)
        v.memset(sa[:, 0:1], 0.0)
        v.memset(sbf[:, 0:1], 0.0)
        v.tensor_scalar(cols[:, COL_W * NC:COL_W * NC + NC], cols[:, COL_W * NC:COL_W * NC + NC], 0.0, None, ALU.max)
        ctx.inc(v.tensor_tensor(cols[:, COL_BKU * NC:COL_BKU * NC + NC],
                                cols[:, COL_BK * NC:COL_BK * NC + NC],
                                cols[:, COL_U * NC:COL_U * NC + NC], ALU.add), "prep", 1)

    def load_w(tag, wdram, dstl, prev_free=None):
        chunks = [wdram[c * 128:(c + 1) * 128, :] for c in range(NC)]
        emit_weight_load(ctx, block, tag, chunks, dstl, wstg,
                         pe_free_count=(prev_free[1] if prev_free else None),
                         pe_free_name=(prev_free[0] if prev_free else None), prev_cast=ctx.lastcast)
        ctx.lastcast = (f"{tag}wc", NC)

    for bb in range(BPC):
        load_w(f"k{bb}", wkT2, wA, prev_free=(None if bb == 0 else ("zmm", ctx.v("zmm"))))
        load_w(f"v{bb}", wvT2, wB, prev_free=(None if bb == 0 else ("atmm", ctx.v("atmm"))))

        @block.sync
        def _(sync, bb=bb):
            if bb > 0:
                ctx.wait(sync, "zmm", ctx.v("zmm"))
            for c in range(NC):
                ctx.inc(sync.dma_start(xn1[c][:, :], xn1fm[bb, c, :, :]), f"lx{bb}", 16)

        for d in range(NC):
            @block.tensor
            def _(t, bb=bb, d=d):
                if d == 0:
                    ctx.wait(t, f"k{bb}wc", NC)
                    ctx.wait(t, f"v{bb}wc", NC)
                    ctx.wait(t, f"lx{bb}", 16 * NC)
                    ctx.wait(t, "prep", 1)
                for sl in range(4):
                    zi = ctx.v("zmm") % 2
                    if ctx.v("zmm") >= 2:
                        ctx.wait(t, "zev", ctx.v("zmm") - 1)
                    for c in range(NC):
                        mi = t.matmul(ps_z[zi][:, :], wA[c][:, d * 128:(d + 1) * 128],
                                      xn1[c][:, 1 + sl * 512:1 + (sl + 1) * 512],
                                      start=(c == 0), stop=(c == NC - 1))
                    ctx.inc(mi, "zmm", 1)
                for sl in range(4):
                    zi = ctx.v("zmm") % 2
                    ctx.wait(t, "zev", ctx.v("zmm") - 1)
                    for c in range(NC):
                        mi = t.matmul(ps_z[zi][:, :], wB[c][:, d * 128:(d + 1) * 128],
                                      xn1[c][:, 1 + sl * 512:1 + (sl + 1) * 512],
                                      start=(c == 0), stop=(c == NC - 1))
                    ctx.inc(mi, "zmm", 1)

            @block.scalar
            def _(s, bb=bb, d=d):
                base = ctx.v("zev")
                for sl in range(4):
                    n = base + sl
                    ctx.wait(s, "zmm", n + 1)
                    if sl == 0:
                        ctx.wait(s, "dv1", ctx.v("dv1"))
                        ctx.wait(s, "rstt", ctx.v("rstt"))
                    ctx.inc(s.activation(zk[:, 1 + sl * 512:1 + (sl + 1) * 512], ps_z[n % 2][:, :], AF.Copy), "zev", 1)
                for sl in range(4):
                    n = base + 4 + sl
                    ctx.wait(s, "zmm", n + 1)
                    if sl == 0:
                        ctx.wait(s, "dkdone", ctx.v("dkdone") + 1)
                    ctx.inc(s.activation(zk[:, 1 + sl * 512:1 + (sl + 1) * 512], ps_z[n % 2][:, :], AF.Copy), "zev", 1)
                ctx.wait(s, "wst", ctx.v("wst"))
                ctx.wait(s, "earg", ctx.v("earg") + 2)
                ctx.inc(s.activation(e_b[:, :], gb[:, :], AF.Exp), "eexp", 1)
                ctx.wait(s, "euarg", ctx.v("euarg") + 2)
                ctx.inc(s.activation(eu_b[:, :], gb[:, :], AF.Exp), "euexp", 1)
                ctx.wait(s, "bdone", ctx.v("bdone") + 1)
                s.activation(sa[:, 1:T + 1], gb[:, :], AF.Ln)
                ctx.inc(s.activation(sbf[:, 1:T + 1], sa[:, 1:T + 1], AF.Exp, scale=-1.0), "invb", 1)

            @block.vector
            def _(v, bb=bb, d=d):
                ctx.wait(v, "zev", ctx.v("zev") - 4)
                v.memset(zk[:, 0:1], 0.0)
                v.memset(zk[:, T + 1:T + 2], 0.0)
                ctx.inc(v.tensor_tensor(dkv[:, :], zk[:, 2:T + 2], zk[:, 0:T], ALU.subtract), "dk0", 1)
                ctx.wait(v, "dk0", ctx.v("dk0"))
                ctx.inc(v.scalar_tensor_tensor(t1b[:, :], dkv[:, :], (1.0 - EPS_K1), zk[:, 0:T], ALU.mult, ALU.add), "dkdone", 1)
                ctx.wait(v, "dkdone", ctx.v("dkdone"))
                ctx.inc(v.scalar_tensor_tensor(ga[:, :], ntr[:, :], col(COL_W, d), t1b[:, :], ALU.mult, ALU.add), "ea0", 1)
                ctx.wait(v, "ea0", ctx.v("ea0"))
                ctx.inc(v.tensor_scalar(gb[:, :], ga[:, :], col(COL_BK, d), -20.0, ALU.add, ALU.max), "ea1", 1)
                ctx.wait(v, "ea1", ctx.v("ea1"))
                ctx.inc(v.tensor_scalar(gb[:, :], gb[:, :], 10.0, None, ALU.min), "earg", 2)
                ctx.wait(v, "eexp", ctx.v("eexp"))
                ctx.inc(v.tensor_scalar(gb[:, :], t1b[:, :], col(COL_BKU, d), -20.0, ALU.add, ALU.max), "eu1", 1)
                ctx.wait(v, "eu1", ctx.v("eu1"))
                ctx.inc(v.tensor_scalar(gb[:, :], gb[:, :], 10.0, None, ALU.min), "euarg", 2)
                ctx.wait(v, "zev", ctx.v("zev"))
                v.memset(zk[:, 0:1], 0.0)
                v.memset(zk[:, T + 1:T + 2], 0.0)
                ctx.inc(v.tensor_tensor(dkv[:, :], zk[:, 2:T + 2], zk[:, 0:T], ALU.subtract), "dv0", 1)
                ctx.wait(v, "dv0", ctx.v("dv0"))
                ctx.inc(v.scalar_tensor_tensor(vfin[:, :], dkv[:, :], (1.0 - EPS_V1), zk[:, 0:T], ALU.mult, ALU.add), "dv05", 1)
                ctx.wait(v, "dv05", ctx.v("dv05"))
                ctx.inc(v.tensor_scalar(vfin[:, :], vfin[:, :], col(COL_BV, d), None, ALU.add), "dv1", 1)
                ctx.wait(v, "euexp", ctx.v("euexp"))
                ctx.wait(v, "dv1", ctx.v("dv1"))
                ctx.inc(v.tensor_tensor(ev_b[:, :], e_b[:, :], vfin[:, :], ALU.mult), "evd", 1)
                ctx.wait(v, "evd", ctx.v("evd"))
                ctx.inc(v.tensor_tensor_scan(sa[:, 1:T + 1], ones2k[:, :], ev_b[:, :], 0.0, ALU.mult, ALU.add), "sca", 1)
                ctx.wait(v, "sca", ctx.v("sca"))
                ctx.inc(v.tensor_tensor(ev_b[:, :], eu_b[:, :], vfin[:, :], ALU.mult), "euvd", 1)
                ctx.inc(v.tensor_tensor_scan(sbf[:, 1:T + 1], ones2k[:, :], e_b[:, :], 0.0, ALU.mult, ALU.add), "scb", 1)
                ctx.wait(v, "euvd", ctx.v("euvd"))
                ctx.inc(v.tensor_tensor(ga[:, :], sa[:, 0:T], ev_b[:, :], ALU.add), "adone", 1)
                ctx.wait(v, "scb", ctx.v("scb"))
                ctx.inc(v.tensor_tensor(gb[:, :], sbf[:, 0:T], eu_b[:, :], ALU.add), "bdone", 1)
                ctx.wait(v, "invb", ctx.v("invb"))
                ctx.inc(v.tensor_tensor(e_b[:, :], ga[:, :], sbf[:, 1:T + 1], ALU.mult), "wkvdn", 1)

            @block.gpsimd
            def _(g_, bb=bb, d=d):
                ctx.wait(g_, "wkvdn", ctx.v("wkvdn"))
                ctx.inc(g_.dma_start(wkvd[bb, d, :, :], e_b[:, :]), "wst", 16)

        load_w(f"r{bb}", wrT2, wB, prev_free=("zmm", ctx.v("zmm")))
        for d in range(NC):
            @block.tensor
            def _(t, bb=bb, d=d):
                if d == 0:
                    ctx.wait(t, f"r{bb}wc", NC)
                for sl in range(4):
                    zi = ctx.v("zmm") % 2
                    ctx.wait(t, "zev", ctx.v("zmm") - 1)
                    for c in range(NC):
                        mi = t.matmul(ps_z[zi][:, :], wB[c][:, d * 128:(d + 1) * 128],
                                      xn1[c][:, 1 + sl * 512:1 + (sl + 1) * 512],
                                      start=(c == 0), stop=(c == NC - 1))
                    ctx.inc(mi, "zmm", 1)

            @block.scalar
            def _(s, bb=bb, d=d):
                base = ctx.v("zev")
                for sl in range(4):
                    n = base + sl
                    ctx.wait(s, "zmm", n + 1)
                    if sl == 0:
                        if d == 0:
                            ctx.wait(s, "dv1", ctx.v("dv1"))
                        else:
                            ctx.wait(s, "rstt", ctx.v("rstt"))
                    ctx.inc(s.activation(zk[:, 1 + sl * 512:1 + (sl + 1) * 512], ps_z[n % 2][:, :], AF.Copy), "zev", 1)
                ctx.wait(s, "wst", 16 * NC * (bb + 1))
                ctx.wait(s, "rstt", ctx.v("rstt") + 1)
                ctx.inc(s.activation(e_b[:, :], t1b[:, :], AF.Sigmoid, bias=col(COL_BR, d)), "sigr", 1)

            @block.vector
            def _(v, bb=bb, d=d):
                ctx.wait(v, "zev", ctx.v("zev"))
                v.memset(zk[:, 0:1], 0.0)
                v.memset(zk[:, T + 1:T + 2], 0.0)
                ctx.inc(v.tensor_tensor(dkv[:, :], zk[:, 2:T + 2], zk[:, 0:T], ALU.subtract), "rd0", 1)
                ctx.wait(v, "rd0", ctx.v("rd0"))
                ctx.inc(v.scalar_tensor_tensor(t1b[:, :], dkv[:, :], (1.0 - EPS_R1), zk[:, 0:T], ALU.mult, ALU.add), "rstt", 1)
                ctx.wait(v, "sigr", ctx.v("sigr"))
                ctx.wait(v, "lwkv", 16 * (bb * NC + d + 1))
                if bb > 0:
                    ctx.wait(v, "atmm", bb * NT)
                ctx.inc(v.tensor_tensor(outpre[:, d * T:(d + 1) * T], e_b[:, :], eu_b[:, :], ALU.mult), "opre", 1)

            @block.sync
            def _(sync, bb=bb, d=d):
                ctx.wait(sync, "wst", 16 * NC * (bb + 1))
                if bb * NC + d >= 1:
                    ctx.wait(sync, "opre", bb * NC + d)
                ctx.inc(sync.dma_start(eu_b[:, :], wkvd[bb, d, :, :]), "lwkv", 16)

        load_w(f"o{bb}", woT2, wB, prev_free=("zmm", ctx.v("zmm")))

        @block.tensor
        def _(t, bb=bb):
            ctx.wait(t, f"o{bb}wc", NC)
            ctx.wait(t, "opre", (bb + 1) * NC)
            for tt in range(NT):
                n = bb * NT + tt
                if n >= 2:
                    ctx.wait(t, "atev", n - 1)
                for hN in range(2):
                    for c in range(NC):
                        mi = t.matmul(ps_a[n % 2][:, hN * 512:(hN + 1) * 512],
                                      outpre[:, c * T + tt * 128:c * T + (tt + 1) * 128],
                                      wB[c][:, hN * 512:(hN + 1) * 512],
                                      start=(c == 0), stop=False)
                    mi = t.matmul(ps_a[n % 2][:, hN * 512:(hN + 1) * 512], ones_r[:, :],
                                  bor_s[:, hN * 512:(hN + 1) * 512], start=False, stop=True)
                ctx.inc(mi, "atmm", 1)

        @block.vector
        def _(v, bb=bb):
            for tt in range(NT):
                n = bb * NT + tt
                ctx.wait(v, "atmm", n + 1)
                if n >= 2:
                    ctx.wait(v, "atst", 16 * (n - 1))
                ctx.inc(v.tensor_copy(att_t[n % 2][:, :], ps_a[n % 2][:, :]), "atev", 1)

        @block.gpsimd
        def _(g_, bb=bb):
            for tt in range(NT):
                n = bb * NT + tt
                ctx.wait(g_, "atev", n + 1)
                ctx.inc(g_.dma_start(att1[bb, tt, :, :], att_t[n % 2][:, :]), "atst", 16)

    @block.gpsimd
    def _(g_):
        ctx.wait(g_, "atst")



def build_E():
    nc = bass.Bass("TRN2", target_bir_lowering=False)
    es = contextlib.ExitStack()
    d = {}
    decl_E(nc, d)
    with nc.Block() as block:
        ctx = Ctx(nc, es)
        ctx.ncblock = block
        emit_E(nc, ctx, block, d)
    return nc, es



def decl_F(nc, d):
    if "xn1fm" not in d:
        d["xn1fm"] = nc.dram_tensor("xn1fm", [BPC, NC, 128, T + 2], BF16, kind="ExternalInput")
    if "x1" not in d:
        d["x1"] = nc.dram_tensor("x1", [BPC, NT, 128, D], F32, kind="ExternalInput")
    if "att1" not in d:
        d["att1"] = nc.dram_tensor("att1", [BPC, NT, 128, D], F32, kind="ExternalInput")
    if "wrT3" not in d:
        d["wrT3"] = nc.dram_tensor("wrT3", [D, D], F32, kind="ExternalInput")
    if "wkT3" not in d:
        d["wkT3"] = nc.dram_tensor("wkT3", [D, D], F32, kind="ExternalInput")
    if "wvT3" not in d:
        d["wvT3"] = nc.dram_tensor("wvT3", [D, D], F32, kind="ExternalInput")
    if "mur_c" not in d:
        d["mur_c"] = nc.dram_tensor("mur_c", [128, NC], F32, kind="ExternalInput")
    if "muk_c" not in d:
        d["muk_c"] = nc.dram_tensor("muk_c", [128, NC], F32, kind="ExternalInput")
    if "outd" not in d:
        d["outd"] = nc.dram_tensor("outd", [BPC, T, D], F32, kind="ExternalOutput")


def emit_F(nc, ctx, block, d):
    xn1fm = d["xn1fm"]
    x1 = d["x1"]
    att1 = d["att1"]
    wrT3 = d["wrT3"]
    wkT3 = d["wkT3"]
    wvT3 = d["wvT3"]
    mur_c = d["mur_c"]
    muk_c = d["muk_c"]
    outd = d["outd"]

    sb = ctx.sb
    xn1 = [sb(f"fxn{c}", [128, T + 2], BF16) for c in range(NC)]
    xk2 = [sb(f"fxk{c}", [128, T], BF16) for c in range(NC)]
    xr2 = [sb(f"fxr{c}", [128, T], BF16) for c in range(NC)]
    kact = [sb(f"fka{d}", [128, T], BF16) for d in range(NC)]
    w0 = [sb(f"fw0{c}", [128, D], BF16) for c in range(NC)]
    w1 = [sb(f"fw1{c}", [128, D], BF16) for c in range(NC)]
    w2 = [sb(f"fw2{c}", [128, D], BF16) for c in range(NC)]
    wstg = [sb("fwst0", [128, D], F32)]
    mcols = sb("fmc", [128, 2 * NC], F32)
    g1 = [sb("fg10", [128, 512], F32), ]
    kv_t = sb("fkv", [128, D], BF16)
    r_t = sb("fr", [128, D], BF16)
    x1t = sb("fx1", [128, D], F32)
    a1t = sb("fa1", [128, D], F32)
    f1t = sb("ff1", [128, D], F32)
    o_t = sb("fo", [128, D], F32)

    ps_z = [ctx.ps(f"fpz{i}", [128, 512], F32) for i in range(2)]
    ps_kv = ctx.ps("fpkv", [128, D], F32)
    ps_r = ctx.ps("fpr", [128, D], F32)

    ctx.lastcast = None

    @block.sync
    def _(sync):
        ctx.inc(sync.dma_start(mcols[:, 0:NC], mur_c[:, :]), "ld0", 16)
        ctx.inc(sync.dma_start(mcols[:, NC:2 * NC], muk_c[:, :]), "ld0", 16)

    def load_w(tag, wdram, dstl):
        chunks = [wdram[c * 128:(c + 1) * 128, :] for c in range(NC)]
        emit_weight_load(ctx, block, tag, chunks, dstl, wstg, prev_cast=ctx.lastcast)
        ctx.lastcast = (f"{tag}wc", NC)

    load_w("k3", wkT3, w0)
    load_w("v3", wvT3, w1)
    load_w("r3", wrT3, w2)

    for bb in range(BPC):
        @block.sync
        def _(sync, bb=bb):
            if bb > 0:
                ctx.wait(sync, "xk2d", ctx.v("xk2d"))
            for c in range(NC):
                ctx.inc(sync.dma_start(xn1[c][:, :], xn1fm[bb, c, :, :]), f"lx{bb}", 16)

        @block.vector
        def _(v, bb=bb):
            ctx.wait(v, f"lx{bb}", 16 * NC)
            ctx.wait(v, "ld0")
            if bb > 0:
                ctx.wait(v, "kmm", ctx.v("kmm"))
                ctx.wait(v, "rmm", ctx.v("rmm"))
            for c in range(NC):
                ctx.inc(v.tensor_tensor(xk2[c][:, :], xn1[c][:, 1:T + 1], xn1[c][:, 0:T], ALU.subtract), "d2", 1)
                ctx.wait(v, "d2", ctx.v("d2"))
                ctx.inc(v.scalar_tensor_tensor(xr2[c][:, :], xk2[c][:, :], mcols[:, c:c + 1], xn1[c][:, 0:T], ALU.mult, ALU.add), "xr2d", 1)
                ctx.wait(v, "xr2d", ctx.v("xr2d"))
                ctx.inc(v.scalar_tensor_tensor(xk2[c][:, :], xk2[c][:, :], mcols[:, NC + c:NC + c + 1], xn1[c][:, 0:T], ALU.mult, ALU.add), "xk2d", 1)
                ctx.wait(v, "xk2d", ctx.v("xk2d"))

        for d in range(NC):
            @block.tensor
            def _(t, bb=bb, d=d):
                if d == 0:
                    ctx.wait(t, "k3wc", NC)
                    ctx.wait(t, "xk2d", (bb + 1) * NC)
                for sl in range(4):
                    n = ctx.v("kmm")
                    if n >= 2:
                        ctx.wait(t, "kev", n - 1)
                    for c in range(NC):
                        mi = t.matmul(ps_z[n % 2][:, :], w0[c][:, d * 128:(d + 1) * 128],
                                      xk2[c][:, sl * 512:(sl + 1) * 512],
                                      start=(c == 0), stop=(c == NC - 1))
                    ctx.inc(mi, "kmm", 1)

            @block.vector
            def _(v, bb=bb, d=d):
                for sl in range(4):
                    n = ctx.v("krl")
                    ctx.wait(v, "kmm", n + 1)
                    if n >= 1:
                        ctx.wait(v, "kev", n)
                    ctx.inc(v.tensor_scalar(g1[0][:, :], ps_z[n % 2][:, :], 0.0, None, ALU.max), "krl", 1)

            @block.scalar
            def _(s, bb=bb, d=d):
                for sl in range(4):
                    n = ctx.v("kev")
                    ctx.wait(s, "krl", n + 1)
                    ctx.inc(s.activation(kact[d][:, sl * 512:(sl + 1) * 512], g1[0][:, :], AF.Square), "kev", 1)

        @block.tensor
        def _(t, bb=bb):
            ctx.wait(t, "v3wc", NC)
            ctx.wait(t, "r3wc", NC)
            ctx.wait(t, "kev", 4 * NC * (bb + 1))
            for tt in range(NT):
                n = bb * NT + tt
                if n >= 1:
                    ctx.wait(t, "kvev", n)
                for hN in range(2):
                    for dch in range(NC):
                        mi = t.matmul(ps_kv[:, hN * 512:(hN + 1) * 512],
                                      kact[dch][:, tt * 128:(tt + 1) * 128],
                                      w1[dch][:, hN * 512:(hN + 1) * 512],
                                      start=(dch == 0), stop=(dch == NC - 1))
                ctx.inc(mi, "kvmm", 1)
                if n >= 1:
                    ctx.wait(t, "rev", n)
                for hN in range(2):
                    for c in range(NC):
                        mi = t.matmul(ps_r[:, hN * 512:(hN + 1) * 512],
                                      xr2[c][:, tt * 128:(tt + 1) * 128],
                                      w2[c][:, hN * 512:(hN + 1) * 512],
                                      start=(c == 0), stop=(c == NC - 1))
                ctx.inc(mi, "rmm", 1)

        @block.scalar
        def _(s, bb=bb):
            for tt in range(NT):
                n = bb * NT + tt
                ctx.wait(s, "rmm", n + 1)
                ctx.wait(s, "ffd", n)
                ctx.inc(s.activation(r_t[:, :], ps_r[:, :], AF.Sigmoid), "rev", 1)

        @block.vector
        def _(v, bb=bb):
            for tt in range(NT):
                n = bb * NT + tt
                ctx.wait(v, "kvmm", n + 1)
                ctx.inc(v.tensor_copy(kv_t[:, :], ps_kv[:, :]), "kvev", 1)
                ctx.wait(v, "lx1", 16 * (n + 1))
                ctx.wait(v, "la1", 16 * (n + 1))
                ctx.inc(v.tensor_tensor(f1t[:, :], x1t[:, :], a1t[:, :], ALU.add), "xa", 1)
                ctx.wait(v, "rev", n + 1)
                ctx.wait(v, "kvev", ctx.v("kvev"))
                ctx.inc(v.tensor_tensor(kv_t[:, :], r_t[:, :], kv_t[:, :], ALU.mult), "ffd", 1)
                ctx.wait(v, "ffd", ctx.v("ffd"))
                if n >= 1:
                    ctx.wait(v, "ost", 16 * n)
                ctx.inc(v.tensor_tensor(o_t[:, :], f1t[:, :], kv_t[:, :], ALU.add), "od", 1)

        @block.sync
        def _(sync, bb=bb):
            for tt in range(NT):
                n = bb * NT + tt
                if n >= 1:
                    ctx.wait(sync, "xa", n)
                ctx.inc(sync.dma_start(x1t[:, :], x1[bb, tt, :, :]), "lx1", 16)
                ctx.inc(sync.dma_start(a1t[:, :], att1[bb, tt, :, :]), "la1", 16)

        @block.gpsimd
        def _(g_, bb=bb):
            for tt in range(NT):
                n = bb * NT + tt
                ctx.wait(g_, "od", n + 1)
                ctx.inc(g_.dma_start(outd[bb, tt * 128:(tt + 1) * 128, :], o_t[:, :]), "ost", 16)

    @block.gpsimd
    def _(g_):
        ctx.wait(g_, "ost")



def build_F():
    nc = bass.Bass("TRN2", target_bir_lowering=False)
    es = contextlib.ExitStack()
    d = {}
    decl_F(nc, d)
    with nc.Block() as block:
        ctx = Ctx(nc, es)
        ctx.ncblock = block
        emit_F(nc, ctx, block, d)
    return nc, es

def emit_X2(nc, ctx, block, d):
    ctxo, ctx_all, yscr = d["ctxo"], d["ctx_all"], d["yscr"]

    @block.gpsimd
    def _(g):
        cc = ctx.rec("cc")
        ins_ap = ctxo[:, :, :, :].rearrange("a h t x -> (a h t) x")
        out_ap = ctx_all[:, :, :, :, :].rearrange("n a h p x -> (n a h p) x")
        g.collective_compute("AllGather", ALU.bypass, [list(range(NCORE))],
                             ins=[ins_ap.opt()], outs=[out_ap.opt()]).then_inc(cc[0], 1)
        cc[1] += 1

    @block.sync
    def _(sync):
        ctx.wait(sync, "cc", 1)
        cid = sync.partition_id()
        view3 = ctx_all[:, :, :, :, :].rearrange("n a h p x -> (n a h) p x").rearrange(
            "(t g) p x -> g t p x", g=16)
        for b in range(BPC):
            gdyn = cid * 2 + b
            src = view3[gdyn, :, :, :]
            dst = yscr[b, :, :].rearrange("(t p) x -> t p x", p=128)
            ctx.inc(sync.dma_start(dst, src), "ys", 16)

    @block.gpsimd
    def _(g):
        ctx.wait(g, "ys")


def build_fused():
    nc = bass.Bass("TRN2", target_bir_lowering=False)
    es = contextlib.ExitStack()
    d = {}
    ikeys = _FLIM.get("int_keys")
    def _kind(key):
        if ikeys is not None:
            return "Internal" if key in ikeys else "ExternalOutput"
        return "ExternalOutput" if _FLIM.get("dbgio") else "Internal"
    ik = None
    d["xn0fm"] = nc.dram_tensor("xn0fm", [BPC, NC, 128, T], BF16, kind=_kind("xn0fm"))
    d["qtf"] = nc.dram_tensor("qtf", [BPC, NC, 128, T], BF16, kind=_kind("qtf"))
    d["ktf"] = nc.dram_tensor("ktf", [BPC, NC, 128, T], BF16, kind=_kind("ktf"))
    d["vaug"] = nc.dram_tensor("vaug", [BPC, NT, 128, NH * 65], BF16, kind=_kind("vaug"))
    d["ctxo"] = nc.dram_tensor("ctxo", [BPC, NH, T, HD], BF16,
                               kind="ExternalInput" if _FLIM.get("xio") else "Internal")
    d["ctx_all"] = nc.dram_tensor("ctx_all", [NCORE, BPC, NH, 128, 16 * HD], BF16, kind="Internal")
    d["yscr"] = nc.dram_tensor("yscr", [BPC, T, D], BF16,
                               kind="ExternalOutput" if (_FLIM.get("xio") or _FLIM.get("dbgio") or (ikeys is not None and "yscr" not in ikeys)) else "Internal")
    d["x1p"] = nc.dram_tensor("x1p", [BPC, NT, 128, D], F32, kind=_kind("x1p"))
    d["x1"] = nc.dram_tensor("x1", [BPC, NT, 128, D], F32, kind=_kind("x1"))
    d["xn1fm"] = nc.dram_tensor("xn1fm", [BPC, NC, 128, T + 2], BF16, kind=_kind("xn1fm"))
    d["att1"] = nc.dram_tensor("att1", [BPC, NT, 128, D], F32, kind=_kind("att1"))
    for f in (decl_A, decl_B, decl_C, decl_D, decl_E, decl_F):
        f(nc, d)
    with nc.Block() as block:
        ctx = Ctx(nc, es)
        ctx.ncblock = block
        allst = {"A": ("sA", emit_A), "B": ("sB", emit_B), "X": ("sX", emit_X2),
                 "C": ("sC", emit_C), "D": ("sD", emit_D), "E": ("sE", emit_E), "F": ("sF", emit_F)}
        stages = [allst[k] for k in _FLIM.get("stages", "ABXCDEF")]
        for idx, (pfx, em) in enumerate(stages):
            ctx.stage_begin(idx, pfx)
            em(nc, ctx, block, d)
            ctx.stage_end()
    return nc, es


def _in_map(h, r):
    return {
        "xin": np.ascontiguousarray(h["x"][2 * r:2 * r + 2]),
        "wqT": h["wqT"], "wkT": h["wkT"], "wvT": h["wvT"],
        "bq_c": h["bq_c"], "bk_c": h["bk_c"], "bv_r": h["bv_r"], "ident_bf": h["ident_bf"],
        "tpT": h["tpT"], "etg": h["etg"], "isg": h["isg"],
        "router_w": h["router_w"], "w_gate": h["w_gate"], "w_up": h["w_up"], "w_down": h["w_down"],
        "ws_gate": h["ws_gate"], "ws_up": h["ws_up"], "ws_down": h["ws_down"],
        "woT": h["woT"], "bo_r": h["bo_r"],
        "wrT2": h["wrT2"], "wkT2": h["wkT2"], "wvT2": h["wvT2"], "woT2": h["woT2"],
        "br_c2": h["br_c2"], "bk_c2": h["bk_c2"], "bv_c2": h["bv_c2"], "bo_r2": h["bo_r2"],
        "td_c": h["td_c"], "u_c": h["u_c"], "trem_row": h["trem_row"],
        "wrT3": h["wrT3"], "wkT3": h["wkT3"], "wvT3": h["wvT3"],
        "mur_c": h["mur_c"], "muk_c": h["muk_c"],
    }


def kernel(**inputs):
    h = _prep_host(inputs)
    nc_es = build_fused()
    res = _run_launch(nc_es, [_in_map(h, r) for r in range(NCORE)])
    out = np.concatenate([np.asarray(res[r]["outd"]) for r in range(NCORE)], axis=0)
    return out.astype(np.float32)


def _run_launch(nc_es, in_maps):
    nc, es = nc_es
    res = run_bass_kernel_spmd(nc, in_maps, core_ids=list(range(len(in_maps))))
    return res.results


def kernel_multi(**inputs):
    h = _prep_host(inputs)
    R = list(range(NCORE))
    xsl = [np.ascontiguousarray(h["x"][2 * r:2 * r + 2]) for r in R]

    resA = _run_launch(build_A(), [
        {"xin": xsl[r], "wqT": h["wqT"], "wkT": h["wkT"], "wvT": h["wvT"],
         "bq_c": h["bq_c"], "bk_c": h["bk_c"], "bv_r": h["bv_r"], "ident_bf": h["ident_bf"]}
        for r in R])

    resB = _run_launch(build_B(), [
        {"qtf": np.asarray(resA[r]["qtf"]), "ktf": np.asarray(resA[r]["ktf"]),
         "vaug": np.asarray(resA[r]["vaug"]), "tpT": h["tpT"], "etg": h["etg"], "isg": h["isg"]}
        for r in R])

    ctxg = np.stack([np.asarray(resB[r]["ctxo"]) for r in R]).reshape(B, NH, T, HD)
    Y = np.ascontiguousarray(
        ctxg.reshape(B, NH, T // 16, 16 * HD).transpose(1, 0, 2, 3).reshape(NH, B * (T // 16), D))

    resC = _run_launch(build_C(), [
        {"xn0fm": np.asarray(resA[r]["xn0fm"]), "xin": xsl[r], "router_w": h["router_w"],
         "w_gate": h["w_gate"], "w_up": h["w_up"], "w_down": h["w_down"],
         "ws_gate": h["ws_gate"], "ws_up": h["ws_up"], "ws_down": h["ws_down"],
         "ident_bf": h["ident_bf"]}
        for r in R])

    resD = _run_launch(build_D(), [
        {"yscr": np.ascontiguousarray(Y[2 * r:2 * r + 2]), "x1p": np.asarray(resC[r]["x1p"]),
         "woT": h["woT"], "bo_r": h["bo_r"], "ident_bf": h["ident_bf"]}
        for r in R])

    resE = _run_launch(build_E(), [
        {"xn1fm": np.asarray(resD[r]["xn1fm"]), "wrT2": h["wrT2"], "wkT2": h["wkT2"],
         "wvT2": h["wvT2"], "woT2": h["woT2"], "br_c2": h["br_c2"], "bk_c2": h["bk_c2"],
         "bv_c2": h["bv_c2"], "bo_r2": h["bo_r2"], "td_c": h["td_c"], "u_c": h["u_c"],
         "trem_row": h["trem_row"]}
        for r in R])

    resF = _run_launch(build_F(), [
        {"xn1fm": np.asarray(resD[r]["xn1fm"]), "x1": np.asarray(resD[r]["x1"]),
         "att1": np.asarray(resE[r]["att1"]), "wrT3": h["wrT3"], "wkT3": h["wkT3"],
         "wvT3": h["wvT3"], "mur_c": h["mur_c"], "muk_c": h["muk_c"]}
        for r in R])

    out = np.concatenate([np.asarray(resF[r]["outd"]) for r in R], axis=0)
    return out.astype(np.float32)
